# version 06 (final)
# speedup vs baseline: 1.4589x; 1.0190x over previous
import numpy as np
from contextlib import ExitStack

import concourse.bass as bass
import concourse.mybir as mybir
from concourse.bass_utils import run_bass_kernel_spmd
from concourse.bass_core_rust import IndirectOffsetOnAxis

F32 = mybir.dt.float32
BF16 = mybir.dt.bfloat16
I32 = mybir.dt.int32
AF = mybir.ActivationFunctionType
ALU = mybir.AluOpType
AX = mybir.AxisListType
ds = bass.ds

D = 2048
NCORE = 8
CH = 512
NCH = 8
TOK = 1024
NT = TOK // 128
HC = D // 128

JCAP = 1280
JPAD = HC * JCAP
NJC = JPAD // 128
JPC = JCAP // 128

L1_I = 8192
L1_IC = 64

NSLOT = 12
EXP_SHIFT = -12.0
DUMMY_BIAS = -40.0

STAGES = ("qkv0", "ffn0", "flash0", "comb0", "qkv1", "ffn1", "flash1", "comb1")
DEBUG_TAPS = ()


class Ctr:
    def __init__(self, nc, es, name, sem=None, base=0):
        self.sem = sem if sem is not None else es.enter_context(nc.semaphore(name))
        self.n = base
        self.base = base
        self.name = name

    def inc(self, instr, k=1):
        instr.then_inc(self.sem, k)
        self.n += k
        return instr

    def wait(self, eng, val=None):
        v = self.n if val is None else self.base + val
        assert v >= self.base, (self.name, val)
        self.maxw = max(getattr(self, "maxw", 0), v)
        if v > 0:
            eng.wait_ge(self.sem, v)

    @property
    def rel(self):
        return self.n - self.base

    def wait_abs(self, eng, val):
        assert val >= 0
        self.maxw = max(getattr(self, "maxw", 0), val)
        if val > 0:
            eng.wait_ge(self.sem, val)

    def check(self):
        assert getattr(self, "maxw", 0) <= self.n, \
            f"ctr {self.name}: waited {self.maxw} > produced {self.n}"


class DStream:
    def __init__(self, P, nbuf=2, per=16, q="sw"):
        self.ctrs = [P.dma_ctr(q) for _ in range(nbuf)]
        self.starts = [c.n for c in self.ctrs]
        self.nbuf = nbuf
        self.per = per

    def put(self, instr, i, k=16):
        self.ctrs[i % self.nbuf].inc(instr, k)
        return instr

    def wait_item(self, eng, i):
        self.ctrs[i % self.nbuf].wait_abs(
            eng, self.starts[i % self.nbuf] + self.per * (i // self.nbuf + 1))

    def wait_all(self, eng):
        for c in self.ctrs:
            c.wait(eng)

    def done(self):
        return [(c, c.n) for c in self.ctrs]


class Prog:
    def __init__(self, nc, es, block):
        self.nc, self.es, self.block = nc, es, block
        self.taps = {}
        self.marks = {}
        self._nctr = 0

    def ctr(self, name=None):
        self._nctr += 1
        if not hasattr(self, "_ctrs"):
            self._ctrs = []
            self._pool = []
            self._protected = set()
            self._phase_start = 0
            self._retire_next = []
        c = Ctr(self.nc, self.es, name or f"c{self._nctr}")
        self._ctrs.append(c)
        return c

    def protect(self, *ctrs):
        self._protected.update(id(c) for c in ctrs)

    def dma_ctr(self, q):
        if not hasattr(self, "_dpool"):
            self._dpool = {"sw": [], "hw": []}
            self._didx = {"sw": 0, "hw": 0}
        pool = self._dpool[q]
        cap = 12
        if len(pool) < cap:
            self._nctr += 1
            c = Ctr(self.nc, self.es, f"d{q}{self._nctr}")
            self._ctrs.append(c)
            self.protect(c)
            pool.append(c)
            c.base = c.n
            return c
        c = pool[self._didx[q] % cap]
        self._didx[q] += 1
        c.base = c.n
        return c

    def sbuf(self, name, shape, dtype):
        return self.es.enter_context(self.nc.sbuf_tensor(name, shape, dtype))

    def psum(self, name, shape, dtype=F32):
        return self.es.enter_context(self.nc.psum_tensor(name, shape, dtype))

    def wait_phase(self, eng):
        for ctr, val in getattr(self, "phase_done", []):
            ctr.wait_abs(eng, val)

    def set_phase_done(self, entries):
        self.phase_done = entries

    def tap(self, name, ap, shape, dtype=F32):
        if name in DEBUG_TAPS:
            dram = self.nc.dram_tensor("TAP_" + name, list(shape), dtype,
                                       kind="ExternalOutput").ap()
            self.taps[name] = (dram, ap)



def _owner(j):
    return (j, 0) if j < 4 else (NCH - 1 - j, 1)


def _host_prepare(inputs):
    ins = {k: np.asarray(v) for k, v in inputs.items()}
    x = ins["L0_attn_x"]

    per_core = [dict() for _ in range(NCORE)]
    for c in range(NCORE):
        b, jq = c // 4, c % 4
        j2 = NCH - 1 - jq
        per_core[c]["XOWN"] = np.ascontiguousarray(np.concatenate(
            [x[b, jq * CH:(jq + 1) * CH, :], x[b, j2 * CH:(j2 + 1) * CH, :]],
            axis=0), dtype=np.float32)

        def mkslots(qchunk, n):
            ks = list(range(qchunk, -1, -1))
            real = len(ks)
            return ks + [0] * (n - real), \
                   [EXP_SHIFT] * real + [DUMMY_BIAS] * (n - real)
        k0, b0v = mkslots(jq, 4)
        k1, b1v = mkslots(j2, 8)
        slots = k0 + k1
        pvec = np.arange(128)
        cols = []
        for NH, HD, kw in ((8, 64, 512), (16, 128, 2048)):
            krows, kcols, vrows = [], [], []
            for kc in slots:
                oc4, pos = _owner(kc)
                oc = b * 4 + oc4
                kcols.append(pos * 512)
                vrows.append(oc * 1024 + pos * 512)
                krows.append(oc * kw)
            for h in range(NH):
                for s in range(NSLOT):
                    if HD == 64:
                        prow = krows[s] + h * HD + (pvec % 64)
                    else:
                        prow = krows[s] + h * HD + pvec
                    cols.append(prow * 2 + kcols[s] // 512)
            if HD == 64:
                for s in range(NSLOT):
                    for sb in range(4):
                        cols.append(vrows[s] + sb * 128 + pvec)
            else:
                for s in range(NSLOT):
                    for sb in range(4):
                        for hh in range(2):
                            vr = vrows[s] + sb * 128 + pvec
                            cols.append(vr * 2 + hh)
        per_core[c]["IDXT"] = np.ascontiguousarray(
            np.stack(cols, axis=1).astype(np.int32))
        bias_cols = np.array(b0v + b1v, dtype=np.float32)
        per_core[c]["BIAST"] = np.ascontiguousarray(
            np.tile(bias_cols[None, :], (128, 1)))

    mask_idx = ins["L0_ffn_mask_idx"].astype(np.int64)
    order = np.argsort(mask_idx, kind="stable")
    midx_s = mask_idx[order]
    rows_s = midx_s // D
    cols_s = midx_s % D
    rowchunk = rows_s // 128
    perm_src = np.zeros(JPAD, dtype=np.int64)
    valid = np.zeros(JPAD, dtype=bool)
    for g in range(HC):
        sel = np.nonzero(rowchunk == g)[0]
        assert len(sel) <= JCAP, f"row-chunk {g}: {len(sel)} > {JCAP}"
        base = g * JCAP
        perm_src[base:base + len(sel)] = sel
        valid[base:base + len(sel)] = True
    atoms_s = np.zeros((64, JPAD), dtype=np.float32)
    imps_s = np.zeros((8, JPAD), dtype=np.float32)
    src = order[perm_src[valid]]
    atoms_s[:, valid] = ins["L0_ffn_atoms"][:, src]
    imps_s[:, valid] = ins["L0_ffn_importance"][:, src]
    cols_pad = np.zeros(JPAD, dtype=np.int32)
    cols_pad[valid] = cols_s[perm_src[valid]]
    onehot = np.zeros((JPAD, 128), dtype=np.float32)
    onehot[np.nonzero(valid)[0], (rows_s[perm_src[valid]] % 128)] = 1.0

    com = {
        "WQ": ins["L0_attn_Wq"].astype(np.float32),
        "WK": ins["L0_attn_Wk"].astype(np.float32),
        "WV": ins["L0_attn_Wv"].astype(np.float32),
        "WO": ins["L0_attn_Wo"].astype(np.float32),
        "BQ": np.ascontiguousarray(ins["L0_attn_bq"].reshape(HC, 128).T.astype(np.float32)),
        "BK": np.ascontiguousarray(ins["L0_attn_bk"].reshape(4, 128).T.astype(np.float32)),
        "BVROW": ins["L0_attn_bv"].reshape(1, 512).astype(np.float32),
        "BO": np.ascontiguousarray(ins["L0_attn_bo"].reshape(HC, 128).T.astype(np.float32)),
        "WT0": np.ascontiguousarray(ins["L0_ffn_W"].T.astype(np.float32)),
        "GWT0": np.ascontiguousarray(ins["L0_ffn_gate_w"].T.astype(np.float32)),
        "EAW": ins["L0_ffn_expert_atom_weights"].astype(np.float32),
        "ATOMS": atoms_s,
        "IMPS": imps_s,
        "ONEHOT": onehot,
        "COLS": np.ascontiguousarray(cols_pad.reshape(NJC, 128).T),
        "WQKVT": np.ascontiguousarray(ins["L1_attn_w_qkv"].T.astype(np.float32)),
        "WOT": np.ascontiguousarray(ins["L1_attn_w_o"].T.astype(np.float32)),
        "GT": np.ascontiguousarray(ins["L1_ffn_gate_w"].T),
        "UT": np.ascontiguousarray(ins["L1_ffn_up_w"].T),
        "DT": np.ascontiguousarray(ins["L1_ffn_down_w"].T),
        "GSC": np.ascontiguousarray(ins["L1_ffn_gate_scale"].reshape(L1_IC, 128).T.astype(np.float32)),
        "USC": np.ascontiguousarray(ins["L1_ffn_up_scale"].reshape(L1_IC, 128).T.astype(np.float32)),
        "DSC": np.ascontiguousarray(ins["L1_ffn_down_scale"].reshape(HC, 128).T.astype(np.float32)),
    }
    return per_core, com



def _declare_io(nc):
    t = {}
    def ei(name, shape, dtype=F32):
        t[name] = nc.dram_tensor(name, list(shape), dtype, kind="ExternalInput").ap()
    ei("XOWN", (TOK, D))
    ei("IDXT", (128, 432), I32)
    ei("BIAST", (128, NSLOT))
    ei("WQ", (D, D)); ei("WK", (D, 512)); ei("WV", (D, 512)); ei("WO", (D, D))
    ei("BQ", (128, HC)); ei("BK", (128, 4)); ei("BVROW", (1, 512)); ei("BO", (128, HC))
    ei("WT0", (D, D)); ei("GWT0", (D, 8)); ei("EAW", (8, 64))
    ei("ATOMS", (64, JPAD)); ei("IMPS", (8, JPAD)); ei("ONEHOT", (JPAD, 128))
    ei("COLS", (128, NJC), I32)
    ei("WQKVT", (D, 3 * D)); ei("WOT", (D, D))
    ei("GT", (D, L1_I), I32); ei("UT", (D, L1_I), I32); ei("DT", (L1_I, D), I32)
    ei("GSC", (128, L1_IC)); ei("USC", (128, L1_IC)); ei("DSC", (128, HC))
    t["OUT"] = nc.dram_tensor("OUT", [TOK, D], F32, kind="ExternalOutput").ap()

    def ii(name, shape, dtype):
        t[name] = nc.dram_tensor(name, list(shape), dtype, kind="Internal").ap()
    ii("XRES", (TOK, D), F32)
    ii("XNFM", (D, TOK), BF16)
    ii("ACT", (L1_I, TOK), BF16)
    ii("CKI0", (512, TOK), BF16); ii("CKO0", (NCORE * 512, TOK), BF16)
    ii("CVI0", (TOK, 512), BF16); ii("CVO0", (NCORE * TOK, 512), BF16)
    ii("CKI1", (D, TOK), BF16); ii("CKO1", (NCORE * D, TOK), BF16)
    ii("CVI1", (TOK, D), BF16); ii("CVO1", (NCORE * TOK, D), BF16)
    return t


def build_program():
    nc = bass.Bass("TRN2", target_bir_lowering=False, num_devices=NCORE)
    io = _declare_io(nc)
    with ExitStack() as es:
        block = es.enter_context(nc.Block())
        P = Prog(nc, es, block)
        P.io = io

        P.xn = P.sbuf("xn_fm", [128, HC * TOK], BF16)
        P.q = P.sbuf("q_fm", [128, HC * TOK], BF16)
        P.g = P.sbuf("g_fm", [128, HC * TOK], BF16)
        P.at = P.sbuf("attn_fm", [128, HC * TOK], BF16)
        P.identb = P.sbuf("identb", [128, 128], BF16)
        P.identf = P.sbuf("identf", [128, 128], F32)
        P.onesb = P.sbuf("onesb", [128, 1], BF16)
        P.onesf = P.sbuf("onesf", [128, 128], F32)
        P.negf = P.sbuf("negf", [1, 128], F32)
        P.epsc = P.sbuf("epsc", [128, 1], F32)
        P.masks = P.sbuf("masks", [128, 4 * 512], BF16)
        P.biast = P.sbuf("biast", [128, NSLOT], F32)
        P.bq = P.sbuf("bq_t", [128, HC], F32)
        P.bk = P.sbuf("bk_t", [128, 4], F32)
        P.bvrow = P.sbuf("bvrow_t", [1, 512], F32)
        P.bo = P.sbuf("bo_t", [128, HC], F32)
        P.gsc = P.sbuf("gsc_t", [128, L1_IC], F32)
        P.usc = P.sbuf("usc_t", [128, L1_IC], F32)
        P.dsc = P.sbuf("dsc_t", [128, HC], F32)
        P.cols = P.sbuf("cols_t", [128, NJC], I32)
        P.csum = P.sbuf("csum", [1, 2 * TOK], F32)

        P.xd = P.ctr("xd")
        P.cc = P.ctr("cc")
        P.st = P.ctr("st")
        P.pre_v = P.ctr("pre_v")
        P.pre_g = P.ctr("pre_g")
        P.protect(P.xd, P.cc, P.st, P.pre_v, P.pre_g)

        _emit_preamble(P)
        _emit_ln(P, 0)
        if "qkv0" in STAGES:
            _emit_qkv0(P)
        if "ffn0" in STAGES:
            _emit_ffn0(P)
        if "flash0" in STAGES:
            _emit_flash(P, 0)
            _emit_oproj(P, 0)
        if "comb0" in STAGES:
            _emit_combine(P, 0)
            _emit_ln(P, 1)
        if "qkv1" in STAGES:
            _emit_qkv1(P)
        if "ffn1" in STAGES:
            _emit_ffn1(P)
        if "flash1" in STAGES:
            _emit_flash(P, 1)
            _emit_oproj(P, 1)
        if "comb1" in STAGES:
            _emit_combine(P, 1)
        _emit_taps(P)
        bad = [(c.name, getattr(c, "maxw", 0), c.n)
               for c in getattr(P, "_ctrs", [])
               if getattr(c, "maxw", 0) > c.n]
        assert not bad, f"ctr audit: {bad}"
    return nc


def _emit_taps(P):
    waits = P.marks.get("final_waits", [])

    @P.block.sync
    def _(sy):
        P.wait_phase(sy)
        for ctr, val in waits:
            ctr.wait_abs(sy, val)
        for name, (dram, ap) in P.taps.items():
            P.st.inc(sy.dma_start(out=dram, in_=ap), 16)
        P.st.wait(sy)



def _emit_preamble(P):
    io = P.io
    P.iota_sc = P.es.enter_context(P.nc.sbuf_tensor("iota_sc", [128, 512], F32))

    @P.block.sync
    def _(sy):
        P.wait_phase(sy)
        for dst, srcn in [(P.bq, "BQ"), (P.bk, "BK"), (P.bvrow, "BVROW"),
                          (P.bo, "BO"), (P.gsc, "GSC"), (P.usc, "USC"),
                          (P.dsc, "DSC"), (P.cols, "COLS"),
                          (P.biast, "BIAST")]:
            P.xd.inc(sy.dma_start(out=dst[:], in_=io[srcn]), 16)

    @P.block.vector
    def _(v):
        P.wait_phase(v)
        v.memset(P.onesb[:], 1.0)
        v.memset(P.onesf[:], 1.0)
        v.memset(P.negf[:], -128.0)
        P.pre_v.inc(v.memset(P.epsc[:], 1e-5))

    @P.block.gpsimd
    def _(g):
        P.wait_phase(g)
        P.pre_g.inc(g.iota(P.iota_sc[0:128, 0:512], pattern=[[1, 512]], base=0,
                           channel_multiplier=-1,
                           allow_small_or_imprecise_dtypes=True))

    @P.block.vector
    def _(v):
        P.wait_phase(v)
        P.pre_g.wait(v)
        for j in range(4):
            P.pre_v.inc(v.tensor_scalar(
                out=P.masks[:, j * 512:(j + 1) * 512], in0=P.iota_sc[:, 0:512],
                scalar1=float(128 * j), scalar2=0.0,
                op0=ALU.subtract, op1=ALU.is_ge))
        P.pre_v.inc(v.tensor_scalar(out=P.identb[:, :], in0=P.iota_sc[:, 0:128],
                                    scalar1=0.0, scalar2=None, op0=ALU.is_equal))
        P.pre_v.inc(v.tensor_scalar(out=P.identf[:, :], in0=P.iota_sc[:, 0:128],
                                    scalar1=0.0, scalar2=None, op0=ALU.is_equal))
    P.set_phase_done([(P.pre_v, P.pre_v.n), (P.pre_g, P.pre_g.n), (P.xd, P.xd.n)])



def _emit_ln(P, layer):
    nc = P.nc
    x_dram = P.io["XOWN"] if layer == 0 else P.io["XRES"]
    name = f"ln{layer}"
    lx = DStream(P, q="hw")
    ls = P.ctr()
    lv = P.ctr()
    lt = P.ctr()
    le = P.ctr()
    lf = P.dma_ctr("hw")
    xn_guard = P.marks.get(f"xnguard{layer}", [])
    res_w = P.marks.get("xresw") if layer == 1 else None

    with (nc.sbuf_tensor(name + "x", [128, 2 * D], F32) as xb,
          nc.sbuf_tensor(name + "sq", [128, 2 * D], F32) as sqb,
          nc.sbuf_tensor(name + "st", [128, 16], F32) as stt,
          nc.sbuf_tensor(name + "xn", [128, 2 * D], BF16) as xnb,
          nc.psum_tensor(name + "tp", [128, 2048], BF16) as tp):

        @P.block.sync
        def _(sy):
            P.wait_phase(sy)
            for i in range(NT):
                if i >= 2:
                    lv.wait(sy, (i - 1) * 5)
                    ls.wait(sy, (i - 1) * 3)
                if res_w is not None:
                    res_w.wait_item(sy, i)
                lx.put(sy.dma_start(
                    out=xb[:, (i % 2) * D:(i % 2 + 1) * D],
                    in_=x_dram[i * 128:(i + 1) * 128, :]), i)

        @P.block.scalar
        def _(s):
            P.wait_phase(s)
            for i in range(NT):
                p = i % 2
                c0 = 8 * p
                lx.wait_item(s, i)
                if i >= 2:
                    ls.wait(s, (i - 2) * 3 + 1)
                    lv.wait(s, (i - 1) * 5)
                ls.inc(s.activation(out=sqb[:, p * D:(p + 1) * D],
                                    in_=xb[:, p * D:(p + 1) * D],
                                    func=AF.Square,
                                    accum_out=stt[:, c0 + 1:c0 + 2]))
                lv.wait(s, i * 5 + 4)
                ls.inc(s.activation(out=stt[:, c0 + 4:c0 + 5],
                                    in_=stt[:, c0 + 3:c0 + 4],
                                    func=AF.Ln, bias=P.epsc[:, 0:1]))
                ls.wait(s, i * 3 + 2)
                ls.inc(s.activation(out=stt[:, c0 + 4:c0 + 5],
                                    in_=stt[:, c0 + 4:c0 + 5],
                                    func=AF.Exp, scale=-0.5))

        @P.block.vector
        def _(v):
            P.wait_phase(v)
            if xn_guard:
                for ctr, val in xn_guard:
                    ctr.wait_abs(v, val)
            for i in range(NT):
                p = i % 2
                c0 = 8 * p
                lx.wait_item(v, i)
                ls.wait(v, i * 3 + 1)
                if i >= 2:
                    lt.wait(v, (i - 1) * 2)
                    lv.wait(v, (i - 1) * 5)
                lv.inc(v.reduce_sum(out=stt[:, c0:c0 + 1],
                                    in_=xb[:, p * D:(p + 1) * D], axis=AX.X))
                lv.wait(v)
                lv.inc(v.tensor_scalar_mul(stt[:, c0 + 2:c0 + 3],
                                           stt[:, c0:c0 + 1], 1.0 / D))
                lv.wait(v)
                lv.inc(v.tensor_mul(stt[:, c0 + 5:c0 + 6], stt[:, c0 + 2:c0 + 3],
                                    stt[:, c0 + 2:c0 + 3]))
                lv.wait(v)
                lv.inc(v.tensor_scalar(
                    out=stt[:, c0 + 3:c0 + 4], in0=stt[:, c0 + 1:c0 + 2],
                    scalar1=1.0 / D, scalar2=stt[:, c0 + 5:c0 + 6],
                    op0=ALU.mult, op1=ALU.subtract))
                ls.wait(v, i * 3 + 3)
                lv.wait(v)
                lv.inc(v.tensor_scalar(
                    out=xnb[:, p * D:(p + 1) * D], in0=xb[:, p * D:(p + 1) * D],
                    scalar1=stt[:, c0 + 2:c0 + 3], scalar2=stt[:, c0 + 4:c0 + 5],
                    op0=ALU.subtract, op1=ALU.mult))
                if i >= 1:
                    for hf in range(2):
                        H = 2 * (i - 1) + hf
                        lt.wait(v, H + 1)
                        for k in range(8):
                            hcg = hf * 8 + k
                            ins = v.tensor_copy(
                                out=P.xn[:, hcg * TOK + (i - 1) * 128:
                                         hcg * TOK + i * 128],
                                in_=tp[:, (H % 2) * 1024 + k * 128:
                                       (H % 2) * 1024 + (k + 1) * 128])
                        le.inc(ins)
            for hf in range(2):
                H = 2 * (NT - 1) + hf
                lt.wait(v, H + 1)
                for k in range(8):
                    hcg = hf * 8 + k
                    ins = v.tensor_copy(
                        out=P.xn[:, hcg * TOK + (NT - 1) * 128:hcg * TOK + NT * 128],
                        in_=tp[:, (H % 2) * 1024 + k * 128:
                               (H % 2) * 1024 + (k + 1) * 128])
                le.inc(ins)

        @P.block.tensor
        def _(t):
            P.wait_phase(t)
            P.pre_v.wait(t)
            for i in range(NT):
                p = i % 2
                lv.wait(t, i * 5 + 5)
                for half in range(2):
                    H = 2 * i + half
                    if H >= 2:
                        le.wait(t, H - 1)
                    for kc in range(8):
                        ins = t.transpose(
                            tp[:, (H % 2) * 1024 + kc * 128:
                               (H % 2) * 1024 + (kc + 1) * 128],
                            xnb[:, p * D + (half * 8 + kc) * 128:
                                p * D + (half * 8 + kc + 1) * 128],
                            P.identb[:, :])
                    lt.inc(ins)

        if layer == 0:
            @P.block.sync
            def _(sy):
                P.wait_phase(sy)
                for i in range(NT):
                    le.wait(sy, 2 * (i + 1))
                    lf.inc(sy.dma_start(
                        out=P.io["XNFM"].rearrange("(hc p) t -> p hc t", p=128)
                            [:, :, i * 128:(i + 1) * 128],
                        in_=P.xn[:].rearrange("p (hc t) -> p hc t", t=TOK)
                            [:, :, i * 128:(i + 1) * 128]), 16)

        P.marks[f"xn{layer}"] = (le, le.n)
        if layer == 0:
            P.marks["xnfm0"] = (lf, lf.n)
        P.protect(le, lf)
        P.set_phase_done([(le, le.n), (lf, lf.n)] + lx.done())
    P.tap(f"xn{layer}", P.xn[:], (128, HC * TOK))



def _emit_qkv0(P):
    nc = P.nc
    io = P.io
    xn_ctr, xn_mark = P.marks["xn0"]
    wp = DStream(P, q="sw")
    kvwA = P.dma_ctr("sw")
    kvwB = P.dma_ctr("sw")
    ev = P.ctr()
    vv = P.ctr()
    kstK = DStream(P, q="hw")
    kstV = DStream(P, q="hw")
    mmq = P.ctr()
    NQ, NK, NV = HC, 4, NT
    with ExitStack() as _es:
        qw = _es.enter_context(nc.sbuf_tensor("q0w", [128, 2 * HC * 128], BF16))
        wk = _es.enter_context(nc.sbuf_tensor("wk0", [128, HC * 512], BF16))
        wv = _es.enter_context(nc.sbuf_tensor("wv0", [128, HC * 512], BF16))
        kstage = _es.enter_context(nc.sbuf_tensor("kst0", [128, 2 * 1024], BF16))
        vstage = _es.enter_context(nc.sbuf_tensor("vst0", [128, 2 * 512], BF16))
        pqa = _es.enter_context(nc.psum_tensor("q0pa", [128, 1024], F32))
        pqb = _es.enter_context(nc.psum_tensor("q0pb", [128, 1024], F32))
        pva = _es.enter_context(nc.psum_tensor("v0pa", [128, 512], F32))
        pvb = _es.enter_context(nc.psum_tensor("v0pb", [128, 512], F32))
        pbv = _es.enter_context(nc.psum_tensor("bv0p", [128, 512], F32))
        P.bvb = _es.enter_context(nc.sbuf_tensor("bvb", [128, 512], F32))
        pq = [pqa, pqb]
        pv = [pva, pvb]

        @P.block.gpsimd
        def _(g):
            P.wait_phase(g)
            kvwA.inc(g.dma_start(
                out=wk[:].rearrange("p (kc m) -> p kc m", m=512),
                in_=io["WK"].rearrange("(kc p) m -> p kc m", p=128)), 16)
            kvwB.inc(g.dma_start(
                out=wv[:].rearrange("p (kc m) -> p kc m", m=512),
                in_=io["WV"].rearrange("(kc p) m -> p kc m", p=128)), 16)
            for m in range(NQ):
                if m >= 2:
                    mmq.wait(g, m - 1)
                wp.put(g.dma_start(
                    out=qw[:, (m % 2) * 2048:(m % 2 + 1) * 2048]
                        .rearrange("p (kc m) -> p kc m", m=128),
                    in_=io["WQ"][:, m * 128:(m + 1) * 128]
                        .rearrange("(kc p) m -> p kc m", p=128)), m)

        @P.block.tensor
        def _(t):
            P.wait_phase(t)
            xn_ctr.wait_abs(t, xn_mark)
            P.pre_v.wait(t)
            t.matmul(pbv[:, :], P.onesf[0:1, 0:128], P.bvrow[0:1, :],
                     start=True, stop=True)
            for m in range(NQ):
                pm = m % 2
                wp.wait_item(t, m)
                if m >= 2:
                    ev.wait(t, m - 1)
                for h2 in range(2):
                    for kc in range(HC):
                        ins = t.matmul(
                            pq[pm][:, h2 * 512:(h2 + 1) * 512],
                            qw[:, pm * 2048 + kc * 128:pm * 2048 + (kc + 1) * 128],
                            P.xn[:, kc * TOK + h2 * 512:kc * TOK + (h2 + 1) * 512],
                            start=(kc == 0), stop=(kc == HC - 1))
                mmq.inc(ins)
            kvwA.wait(t, 16)
            for m4 in range(NK):
                mi = NQ + m4
                pm = mi % 2
                if mi >= 2:
                    ev.wait(t, mi - 1)
                for h2 in range(2):
                    for kc in range(HC):
                        ins = t.matmul(
                            pq[pm][:, h2 * 512:(h2 + 1) * 512],
                            wk[:, kc * 512 + m4 * 128:kc * 512 + (m4 + 1) * 128],
                            P.xn[:, kc * TOK + h2 * 512:kc * TOK + (h2 + 1) * 512],
                            start=(kc == 0), stop=(kc == HC - 1))
                mmq.inc(ins)
            kvwB.wait(t, 16)
            for mt in range(NV):
                pm = mt % 2
                if mt >= 2:
                    vv.wait(t, mt)
                for kc in range(HC):
                    ins = t.matmul(
                        pv[pm][:, :],
                        P.xn[:, kc * TOK + mt * 128:kc * TOK + (mt + 1) * 128],
                        wv[:, kc * 512:(kc + 1) * 512],
                        start=(kc == 0), stop=(kc == HC - 1))
                mmq.inc(ins)

        @P.block.scalar
        def _(s):
            P.wait_phase(s)
            for m in range(NQ):
                pm = m % 2
                mmq.wait(s, m + 1)
                ev.inc(s.activation(out=P.q[:, m * TOK:(m + 1) * TOK],
                                    in_=pq[pm][:, :], func=AF.Identity,
                                    bias=P.bq[:, m:m + 1]))
            for m4 in range(NK):
                mi = NQ + m4
                pm = mi % 2
                mmq.wait(s, mi + 1)
                if m4 >= 2:
                    kstK.wait_item(s, m4 - 2)
                ev.inc(s.activation(
                    out=kstage[:, (m4 % 2) * 1024:(m4 % 2 + 1) * 1024],
                    in_=pq[pm][:, :], func=AF.Identity, bias=P.bk[:, m4:m4 + 1]))

        @P.block.vector
        def _(v):
            P.wait_phase(v)
            mmq.wait(v, 1)
            vv.inc(v.tensor_copy(out=P.bvb[:], in_=pbv[:, :]))
            vv.wait(v, 1)
            for mt in range(NV):
                pm = mt % 2
                mmq.wait(v, NQ + NK + mt + 1)
                if mt >= 2:
                    kstV.wait_item(v, mt - 2)
                vv.inc(v.scalar_tensor_tensor(
                    out=vstage[:, (mt % 2) * 512:(mt % 2 + 1) * 512],
                    in0=pv[pm][:, :], scalar=1.0, in1=P.bvb[:],
                    op0=ALU.mult, op1=ALU.add))

        @P.block.sync
        def _(sy):
            P.wait_phase(sy)
            for m4 in range(NK):
                ev.wait(sy, NQ + m4 + 1)
                kstK.put(sy.dma_start(
                    out=io["CKI0"][m4 * 128:(m4 + 1) * 128, :],
                    in_=kstage[:, (m4 % 2) * 1024:(m4 % 2 + 1) * 1024]), m4)
            for mt in range(NV):
                vv.wait(sy, mt + 2)
                kstV.put(sy.dma_start(
                    out=io["CVI0"][mt * 128:(mt + 1) * 128, :],
                    in_=vstage[:, (mt % 2) * 512:(mt % 2 + 1) * 512]), mt)

        @P.block.gpsimd
        def _(g):
            P.wait_phase(g)
            kstK.wait_all(g)
            kstV.wait_all(g)
            P.cc.inc(g.collective_compute(
                "AllGather", ALU.bypass, replica_groups=[list(range(NCORE))],
                ins=[io["CKI0"].opt()], outs=[io["CKO0"].opt()]), 1)
            P.cc.inc(g.collective_compute(
                "AllGather", ALU.bypass, replica_groups=[list(range(NCORE))],
                ins=[io["CVI0"].opt()], outs=[io["CVO0"].opt()]), 1)

        P.marks["q0"] = (ev, ev.n)
        P.marks["cc0"] = (P.cc, P.cc.n)
        P.protect(ev)
        P.set_phase_done([(mmq, mmq.n), (ev, ev.n), (vv, vv.n)]
                         + kstK.done() + kstV.done())
    P.tap("q0", P.q[:], (128, HC * TOK))



def _emit_ffn0(P):
    nc = P.nc
    io = P.io
    xn_ctr, xn_mark = P.marks["xn0"]
    xnfm_ctr, xnfm_mark = P.marks["xnfm0"]
    lw = P.dma_ctr("sw")
    ip = P.dma_ctr("hw")
    ts = P.ctr()
    sv = P.ctr()
    dv = P.ctr()
    ohp = DStream(P, q="sw")
    xgp = DStream(P, q="sw")
    wtp = DStream(P, q="sw")
    vls = P.ctr()
    sct = P.ctr()
    zv = P.ctr()
    gev = P.ctr()
    with ExitStack() as _es:
        gwt = _es.enter_context(nc.sbuf_tensor("gwt", [128, HC * 8], BF16))
        smT = _es.enter_context(nc.sbuf_tensor("smT", [64, 8], BF16))
        wtt = _es.enter_context(nc.sbuf_tensor("wtt", [8, JPAD], BF16))
        cem = _es.enter_context(nc.sbuf_tensor("cem", [8, TOK], BF16))
        _es1 = _es.enter_context(ExitStack())
        pmisc = _es1.enter_context(nc.psum_tensor("f0m", [128, 512], F32))
        pmb = _es1.enter_context(nc.psum_tensor("f0mb", [128, 128], BF16))
        eaw = _es1.enter_context(nc.sbuf_tensor("eaw", [8, 64], F32))
        eawx = _es1.enter_context(nc.sbuf_tensor("eawx", [8, 64], F32))
        eastat = _es1.enter_context(nc.sbuf_tensor("eastat", [8, 4], F32))
        smb = _es1.enter_context(nc.sbuf_tensor("smb", [8, 64], BF16))
        atm = _es1.enter_context(nc.sbuf_tensor("atm", [64, 2 * 512], BF16))
        imp = _es1.enter_context(nc.sbuf_tensor("imp", [8, 2 * 512], F32))
        sgt = _es1.enter_context(nc.sbuf_tensor("sgt", [8, 2 * 512], F32))
        lgem = _es1.enter_context(nc.sbuf_tensor("lgem", [8, TOK], F32))
        lgtm = _es1.enter_context(nc.sbuf_tensor("lgtm", [128, NT * 8], F32))
        tp2 = _es1.enter_context(nc.sbuf_tensor("tp2", [128, 16], F32))
        cf = _es1.enter_context(nc.sbuf_tensor("cf", [128, 16], F32))
        cbf = _es1.enter_context(nc.sbuf_tensor("cbf", [128, 8], BF16))

        @P.block.gpsimd
        def _(g):
            P.wait_phase(g)
            lw.inc(g.dma_start(
                out=gwt[:].rearrange("p (kc m) -> p kc m", m=8),
                in_=io["GWT0"].rearrange("(kc p) m -> p kc m", p=128)), 16)

        @P.block.sync
        def _(sy):
            P.wait_phase(sy)
            ip.inc(sy.dma_start(out=eaw[:], in_=io["EAW"]), 16)

        @P.block.tensor
        def _(t):
            P.wait_phase(t)
            xn_ctr.wait_abs(t, xn_mark)
            lw.wait(t, 16)
            for h2 in range(2):
                if h2 == 1:
                    dv.wait(t, 1)
                for kc in range(HC):
                    ins = t.matmul(
                        pmisc[0:8, :], gwt[:, kc * 8:(kc + 1) * 8],
                        P.xn[:, kc * TOK + h2 * 512:kc * TOK + (h2 + 1) * 512],
                        start=(kc == 0), stop=(kc == HC - 1))
                ts.inc(ins)

        @P.block.vector
        def _(v):
            P.wait_phase(v)
            for h2 in range(2):
                ts.wait(v, h2 + 1)
                dv.inc(v.tensor_scalar(
                    out=lgem[:, h2 * 512:(h2 + 1) * 512], in0=pmisc[0:8, :],
                    scalar1=50.0, scalar2=-50.0, op0=ALU.min, op1=ALU.max))

        @P.block.tensor
        def _(t):
            P.wait_phase(t)
            dv.wait(t, 2)
            P.pre_v.wait(t)
            for i in range(NT):
                ins = t.transpose(pmisc[:, i * 8:(i + 1) * 8],
                                  lgem[0:8, i * 128:(i + 1) * 128],
                                  P.identf[0:8, 0:8])
            ts.inc(ins)

        @P.block.vector
        def _(v):
            P.wait_phase(v)
            ts.wait(v, 3)
            dv.inc(v.tensor_copy(out=lgtm[:], in_=pmisc[:, 0:64]))

        for i in range(NT):
            d0 = dv.rel
            s0 = sv.rel

            @P.block.vector
            def _(v, i=i, d0=d0, s0=s0):
                P.wait_phase(v)
                dv.wait(v, d0)
                dv.inc(v.max(out=tp2[:, 8:16], in_=lgtm[:, i * 8:(i + 1) * 8]))
                dv.wait(v)
                dv.inc(v.tensor_copy(out=tp2[:, 0:2], in_=tp2[:, 8:10]))
                dv.wait(v)
                dv.inc(v.tensor_sub(tp2[:, 2:3], tp2[:, 1:2], tp2[:, 0:1]))
                sv.wait(v, s0 + 1)
                dv.inc(v.tensor_scalar_add(tp2[:, 4:5], tp2[:, 3:4], 1.0))
                dv.wait(v)
                dv.inc(v.reciprocal(tp2[:, 5:6], tp2[:, 4:5]))
                dv.wait(v)
                dv.inc(v.tensor_mul(tp2[:, 6:7], tp2[:, 3:4], tp2[:, 5:6]))
                v.tensor_scalar(out=cf[:, 0:8], in0=lgtm[:, i * 8:(i + 1) * 8],
                                scalar1=tp2[:, 0:1], scalar2=None,
                                op0=ALU.is_equal)
                dv.inc(v.tensor_scalar(out=cf[:, 8:16],
                                       in0=lgtm[:, i * 8:(i + 1) * 8],
                                       scalar1=tp2[:, 1:2], scalar2=None,
                                       op0=ALU.is_equal))
                dv.wait(v)
                dv.inc(v.tensor_scalar(out=cf[:, 0:8], in0=cf[:, 0:8],
                                       scalar1=tp2[:, 5:6], scalar2=None,
                                       op0=ALU.mult))
                dv.wait(v)
                dv.inc(v.scalar_tensor_tensor(out=cf[:, 0:8], in0=cf[:, 8:16],
                                              scalar=tp2[:, 6:7], in1=cf[:, 0:8],
                                              op0=ALU.mult, op1=ALU.add))
                dv.wait(v)
                dv.inc(v.tensor_copy(out=cbf[:, 0:8], in_=cf[:, 0:8]))

            @P.block.scalar
            def _(s, i=i, d0=d0, s0=s0):
                P.wait_phase(s)
                dv.wait(s, d0 + 3)
                sv.inc(s.activation(out=tp2[:, 3:4], in_=tp2[:, 2:3],
                                    func=AF.Exp))

            @P.block.tensor
            def _(t, i=i, d0=d0):
                P.wait_phase(t)
                dv.wait(t, d0 + 10)
                ts.inc(t.transpose(pmb[0:8, 0:128], cbf[:, 0:8],
                                   P.identb[:, :]))

            @P.block.vector
            def _(v, i=i):
                P.wait_phase(v)
                ts.wait(v)
                dv.inc(v.tensor_copy(out=cem[:, i * 128:(i + 1) * 128],
                                     in_=pmb[0:8, 0:128]))

        @P.block.vector
        def _(v):
            P.wait_phase(v)
            ip.wait(v, 16)
            dv.inc(v.reduce_max(out=eastat[:, 0:1], in_=eaw[:], axis=AX.X))
            dv.wait(v)
            dv.inc(v.tensor_scalar_mul(eastat[:, 1:2], eastat[:, 0:1], -1.0))

        @P.block.scalar
        def _(s):
            P.wait_phase(s)
            dv.wait(s)
            sv.inc(s.activation(out=eawx[:], in_=eaw[:], func=AF.Exp,
                                bias=eastat[:, 1:2]))

        @P.block.vector
        def _(v):
            P.wait_phase(v)
            sv.wait(v)
            dv.inc(v.reduce_sum(out=eastat[:, 2:3], in_=eawx[:], axis=AX.X))
            dv.wait(v)
            dv.inc(v.reciprocal(eastat[:, 3:4], eastat[:, 2:3]))
            dv.wait(v)
            dv.inc(v.tensor_scalar(out=eawx[:], in0=eawx[:],
                                   scalar1=eastat[:, 3:4], scalar2=None,
                                   op0=ALU.mult))
            dv.wait(v)
            dv.inc(v.tensor_copy(out=smb[:], in_=eawx[:]))

        @P.block.tensor
        def _(t):
            P.wait_phase(t)
            dv.wait(t)
            ts.inc(t.transpose(pmb[0:64, 0:8], smb[:, :], P.identb[0:8, 0:8]))

        @P.block.vector
        def _(v):
            P.wait_phase(v)
            ts.wait(v)
            dv.inc(v.tensor_copy(out=smT[:], in_=pmb[0:64, 0:8]))

        NWT = JPAD // 512
        e_ts0, e_sv0, e_dv0 = ts.rel, sv.rel, dv.rel
        atp = DStream(P, q="sw")
        imq = DStream(P, q="hw")

        @P.block.gpsimd
        def _(g):
            P.wait_phase(g)
            for n in range(NWT):
                if n >= 2:
                    ts.wait(g, e_ts0 + n - 1)
                atp.put(g.dma_start(
                    out=atm[0:64, (n % 2) * 512:(n % 2 + 1) * 512],
                    in_=io["ATOMS"][0:64, n * 512:(n + 1) * 512]), n)

        @P.block.sync
        def _(sy):
            P.wait_phase(sy)
            for n in range(NWT):
                if n >= 2:
                    sv.wait(sy, e_sv0 + n - 1)
                imq.put(sy.dma_start(
                    out=imp[0:8, (n % 2) * 512:(n % 2 + 1) * 512],
                    in_=io["IMPS"][0:8, n * 512:(n + 1) * 512]), n)

        @P.block.tensor
        def _(t):
            P.wait_phase(t)
            dv.wait(t, e_dv0)
            for n in range(NWT):
                atp.wait_item(t, n)
                if n >= 1:
                    dv.wait(t, e_dv0 + n)
                ts.inc(t.matmul(pmisc[0:8, :], smT[0:64, :],
                                atm[0:64, (n % 2) * 512:(n % 2 + 1) * 512],
                                start=True, stop=True))

        @P.block.scalar
        def _(s):
            P.wait_phase(s)
            for n in range(NWT):
                imq.wait_item(s, n)
                if n >= 2:
                    dv.wait(s, e_dv0 + n - 1)
                sv.inc(s.activation(out=sgt[0:8, (n % 2) * 512:(n % 2 + 1) * 512],
                                    in_=imp[0:8, (n % 2) * 512:(n % 2 + 1) * 512],
                                    func=AF.Sigmoid))

        @P.block.vector
        def _(v):
            P.wait_phase(v)
            for n in range(NWT):
                ts.wait(v, e_ts0 + n + 1)
                sv.wait(v, e_sv0 + n + 1)
                dv.inc(v.tensor_mul(wtt[0:8, n * 512:(n + 1) * 512],
                                    pmisc[0:8, :],
                                    sgt[0:8, (n % 2) * 512:(n % 2 + 1) * 512]))

        _es1.close()
        pga = _es.enter_context(nc.psum_tensor("f0pa", [128, 1024], F32))
        pgb = _es.enter_context(nc.psum_tensor("f0pb", [128, 1024], F32))
        pva = _es.enter_context(nc.psum_tensor("f0va", [128, 1024], F32))
        pvb = _es.enter_context(nc.psum_tensor("f0vb", [128, 1024], F32))
        w0b = _es.enter_context(nc.sbuf_tensor("w0buf", [128, 2 * HC * 128], BF16))
        ohb = _es.enter_context(nc.sbuf_tensor("ohbuf", [128, 2 * 128], BF16))
        xgb = _es.enter_context(nc.sbuf_tensor("xgbuf", [128, 2 * TOK], BF16))
        zb = _es.enter_context(nc.sbuf_tensor("zbuf", [128, 2 * TOK], BF16))
        pg = [pga, pgb]
        pvals = [pva, pvb]

        f_dv0 = dv.rel

        @P.block.gpsimd
        def _(g):
            P.wait_phase(g)
            xnfm_ctr.wait_abs(g, xnfm_mark)
            sv.wait(g); dv.wait(g); ts.wait(g)
            for gc in range(HC):
                if gc >= 2:
                    gev.wait(g, gc - 1)
                wtp.put(g.dma_start(
                    out=w0b[:, (gc % 2) * 2048:(gc % 2 + 1) * 2048]
                        .rearrange("p (kc m) -> p kc m", m=128),
                    in_=io["WT0"][:, gc * 128:(gc + 1) * 128]
                        .rearrange("(kc p) m -> p kc m", p=128)), gc)
                for jj in range(JPC):
                    jc = gc * JPC + jj
                    if jc >= 2:
                        sct.wait(g, jc - 1)
                    ohp.put(g.dma_start(
                        out=ohb[:, (jc % 2) * 128:(jc % 2 + 1) * 128],
                        in_=io["ONEHOT"][jc * 128:(jc + 1) * 128, :]), jc)
                    xgp.put(g.indirect_dma_start(
                        xgb[:, (jc % 2) * TOK:(jc % 2 + 1) * TOK], None,
                        io["XNFM"][0:D, :],
                        IndirectOffsetOnAxis(P.cols[:, jc:jc + 1], 0)), jc)

        @P.block.tensor
        def _(t):
            P.wait_phase(t)
            dv.wait(t); sv.wait(t); ts.wait(t)

            def emit_vals(jc):
                pj = jc % 2
                if jc >= 2:
                    zv.wait(t, jc - 1)
                ins = None
                for h2 in range(2):
                    ins = t.matmul(pvals[pj][:, h2 * 512:(h2 + 1) * 512],
                                   wtt[0:8, jc * 128:(jc + 1) * 128],
                                   cem[0:8, h2 * 512:(h2 + 1) * 512],
                                   start=True, stop=True)
                vls.inc(ins)

            for gc in range(HC):
                pm = gc % 2
                wtp.wait_item(t, gc)
                if gc >= 2:
                    gev.wait(t, gc - 1)
                if gc == 0:
                    emit_vals(0)
                    emit_vals(1)
                for h2 in range(2):
                    for kc in range(HC):
                        t.matmul(pg[pm][:, h2 * 512:(h2 + 1) * 512],
                                 w0b[:, pm * 2048 + kc * 128:pm * 2048 + (kc + 1) * 128],
                                 P.xn[:, kc * TOK + h2 * 512:kc * TOK + (h2 + 1) * 512],
                                 start=(kc == 0), stop=False)
                for jj in range(JPC):
                    jc = gc * JPC + jj
                    pj = jc % 2
                    zv.wait(t, jc + 1)
                    ohp.wait_item(t, jc)
                    ins = None
                    for h2 in range(2):
                        ins = t.matmul(pg[pm][:, h2 * 512:(h2 + 1) * 512],
                                       ohb[:, pj * 128:(pj + 1) * 128],
                                       zb[:, pj * TOK + h2 * 512:pj * TOK + (h2 + 1) * 512],
                                       start=False, stop=(jj == JPC - 1))
                    sct.inc(ins)
                    if jc + 2 < NJC:
                        emit_vals(jc + 2)

        @P.block.vector
        def _(v):
            P.wait_phase(v)
            sv.wait(v); ts.wait(v)
            for jc in range(NJC):
                pj = jc % 2
                vls.wait(v, jc + 1)
                xgp.wait_item(v, jc)
                if jc >= 2:
                    sct.wait(v, jc - 1)
                zv.inc(v.scalar_tensor_tensor(
                    out=zb[:, pj * TOK:(pj + 1) * TOK],
                    in0=pvals[pj][:, :], scalar=1.0,
                    in1=xgb[:, pj * TOK:(pj + 1) * TOK],
                    op0=ALU.mult, op1=ALU.mult))

        @P.block.scalar
        def _(s):
            P.wait_phase(s)
            for gc in range(HC):
                pm = gc % 2
                sct.wait(s, (gc + 1) * JPC)
                gev.inc(s.activation(out=P.g[:, gc * TOK:(gc + 1) * TOK],
                                     in_=pg[pm][:, :], func=AF.Silu))

        P.marks["g0"] = (gev, gev.n)
        P.marks["xnreads0"] = (sct, sct.n)
        P.protect(gev, sct)
        P.set_phase_done([(gev, gev.n), (zv, zv.n), (dv, dv.n), (sv, sv.n)]
                         + xgp.done() + ohp.done() + wtp.done())
    P.tap("g0", P.g[:], (128, HC * TOK))
    P.tap("cem0", None, (8, TOK))



def _emit_flash(P, layer):
    nc = P.nc
    io = P.io
    KBUF = 4
    if layer == 0:
        NH, HD, scale = 8, 64, 1.0 / 8.0
        KO, VO = io["CKO0"], io["CVO0"]
        kst_p = 64
        q_ctr, q_mark = P.marks["q0"]
        cc_ctr, cc_mark = P.marks["cc0"]
        AUG = HD + 1
        IB, NIDX = 0, 144
        phase_defs = [(h, qc, srange, gh)
                      for h in range(NH)
                      for qc, srange in ((0, range(0, 4)), (1, range(4, NSLOT)))
                      for gh in (0, 1)]
    else:
        NH, HD, scale = 16, 128, float(1.0 / np.sqrt(128.0))
        KO, VO = io["CKO1"], io["CVO1"]
        kst_p = 128
        q_ctr, q_mark = P.marks["q1"]
        cc_ctr, cc_mark = P.marks["cc1"]
        AUG = HD
        IB, NIDX = 144, 288
        sp_defs = [(0, range(0, 4), 0), (0, range(0, 4), 1),
                   (1, range(4, NSLOT), 0), (1, range(4, NSLOT), 1)]
        phase_defs = [(h, qc, srange, 0)
                      for qc, srange, hh in sp_defs
                      for h in range(hh * 8, hh * 8 + 8)]
    GPV = 4 if layer == 0 else 2

    kvp = DStream(P, nbuf=KBUF, per=16, q="sw")
    idxl = P.dma_ctr("hw")
    pe = P.ctr()
    ex = P.ctr()
    mkv = P.ctr()
    pv = P.ctr()
    fz = P.ctr()
    pb2 = P.ctr()
    at_guard = P.marks.get("atread0") if layer == 1 else None

    phases = []
    ginfo = []
    itc = 0
    for pi, (h, qc, srange, gh) in enumerate(phase_defs):
        if layer == 0:
            glist = [[(sb, gh * 2), (sb, gh * 2 + 1)] for sb in range(4)]
        else:
            glist = [[(0, 0), (1, 0)], [(2, 0), (3, 0)]]
        g0 = len(ginfo)
        for s in srange:
            it = itc
            itc += 1
            for gi in range(len(glist)):
                ginfo.append((h, s, it, it % KBUF, qc, pi, gi))
        phases.append((h, qc, srange, g0, len(ginfo), glist, gh))
    NGTOT = len(ginfo)
    NIT = itc
    masked = set(i for i, inf in enumerate(ginfo) if inf[1] in (0, 4))
    mk_before = {}
    cnt = 0
    for i in range(NGTOT):
        if i in masked:
            cnt += 1
        mk_before[i] = cnt
    FPP = 6 if layer == 0 else 3
    FZB = 1 if layer == 0 else 0
    it_list = []
    for h, qc, srange, g0, g1, glist, gh in phases:
        for s in srange:
            it_list.append((h, s))
    if layer == 1:
        sp_first = [phases[spi * 8][3] for spi in range(4)]
        sp_end_pv = [phases[spi * 8 + 7][4] for spi in range(4)]
        sp_tiles = [16, 16, 32, 32]
        sp_cum = [16, 32, 64, 96]

    with (nc.sbuf_tensor(f"fidx{layer}", [128, NIDX], I32) as idxt,
          nc.sbuf_tensor(f"fkst{layer}", [128, KBUF * 512], BF16) as kst,
          nc.sbuf_tensor(f"fpst{layer}", [128, 4 * 1024], BF16) as pst,
          nc.sbuf_tensor(f"frc{layer}", [128, 1024], F32) as rcs,
          nc.sbuf_tensor(f"fbi{layer}", [128, 1024], F32) as bisb,
          nc.psum_tensor(f"sp{layer}a", [128, 1024], F32) as psa,
          ExitStack() as acc_es):
        if layer == 0:
            vraw = acc_es.enter_context(
                nc.sbuf_tensor("vraw0", [128, 4 * 512], BF16))
            vx0 = acc_es.enter_context(
                nc.sbuf_tensor("vx0", [128, 17 * 520], BF16))
            NXT0 = 31
            VS0 = 8 * 65

            def vhome(vt, lo, hi):
                if vt < NXT0:
                    return P.xn[:, vt * VS0 + lo:vt * VS0 + hi]
                return vx0[:, (vt - NXT0) * VS0 + lo:(vt - NXT0) * VS0 + hi]
            psb0 = acc_es.enter_context(nc.psum_tensor("sp0b", [128, 1024], F32))
            ps2 = [psa, psb0]
            accs = [acc_es.enter_context(
                nc.psum_tensor(f"ac0_{g}", [AUG, 512], F32)) for g in range(2)]
            pbi = acc_es.enter_context(nc.psum_tensor("bi0p", [64, 512], F32))
            pbi2 = acc_es.enter_context(nc.psum_tensor("bi0q", [64, 512], F32))
            pbis = [pbi, pbi2]
            vrs = DStream(P, nbuf=4, per=16, q="sw")
            vready = P.ctr()
        else:
            vx1 = acc_es.enter_context(
                nc.sbuf_tensor("vx1", [128, 16 * 1024], BF16))
            XT1 = 16

            def vhome(bt, lo, hi):
                if bt < XT1:
                    return P.xn[:, bt * 1024 + lo:bt * 1024 + hi]
                return vx1[:, (bt - XT1) * 1024 + lo:(bt - XT1) * 1024 + hi]

            def l1_bt(spi, s, sb):
                if spi == 0:
                    return s * 4 + sb
                if spi == 1:
                    return 16 + s * 4 + sb
                return (s - 4) * 4 + sb
            psb = acc_es.enter_context(nc.psum_tensor("sp1b", [128, 1024], F32))
            ps2 = [psa, psb]
            accs = [acc_es.enter_context(nc.psum_tensor(f"ac1_{m}", [128, 512], F32))
                    for m in range(2)]
            pl1 = acc_es.enter_context(nc.psum_tensor("l1p", [1, 512], F32))
            pbi = acc_es.enter_context(nc.psum_tensor("bi1p", [128, 512], F32))
            vgs = P.dma_ctr("sw")
            vg_base = vgs.n

        @P.block.sync
        def _(sy):
            P.wait_phase(sy)
            P.xd.wait(sy)
            idxl.inc(sy.dma_start(out=idxt[:],
                                  in_=io["IDXT"][:, IB:IB + NIDX]), 16)

        KOV = KO.rearrange("r (two c) -> (r two) c", c=512)
        if layer == 1:
            VOV = VO.rearrange("r (hh c) -> (r hh) c", c=1024)

        def put_k(g, it):
            h, s = it_list[it]
            if it >= KBUF:
                pv.wait(g, (it - (KBUF - 1)) * GPV)
            kc_col = h * NSLOT + s
            kvp.put(g.indirect_dma_start(
                kst[0:128, (it % KBUF) * 512:(it % KBUF + 1) * 512], None,
                KOV[:, :],
                IndirectOffsetOnAxis(idxt[:, kc_col:kc_col + 1], 0)), it)

        @P.block.gpsimd
        def _(g):
            P.wait_phase(g)
            idxl.wait(g, 16)
            cc_ctr.wait_abs(g, cc_mark)
            if layer == 0:
                def put_v(vt):
                    if vt >= 4:
                        vready.wait(g, vt - 3)
                    vrs.put(g.indirect_dma_start(
                        vraw[:, (vt % 4) * 512:(vt % 4 + 1) * 512], None,
                        VO[:, :],
                        IndirectOffsetOnAxis(
                            idxt[:, 96 + vt:97 + vt], 0)), vt)
                for it in range(KBUF):
                    put_k(g, it)
                for vt in range(16):
                    put_v(vt)
                for it in range(KBUF, 2 * KBUF):
                    put_k(g, it)
                for vt in range(16, 48):
                    put_v(vt)
                for it in range(2 * KBUF, NIT):
                    put_k(g, it)
            else:
                it0 = 0
                for spi in range(4):
                    qc, srange, hh = sp_defs[spi]
                    n_visit = len(srange) * 8
                    kstart = it0
                    if spi >= 2:
                        for it in range(it0, it0 + KBUF):
                            put_k(g, it)
                        kstart = it0 + KBUF
                        pv.wait(g, sp_end_pv[spi - 1])
                    for s in srange:
                        for sb in range(4):
                            bt = l1_bt(spi, s, sb)
                            vcol = 192 + (s * 4 + sb) * 2 + hh
                            vgs.inc(g.indirect_dma_start(
                                vhome(bt, 0, 1024), None,
                                VOV[:, :],
                                IndirectOffsetOnAxis(
                                    idxt[:, vcol:vcol + 1], 0)), 16)
                    for it in range(kstart, it0 + n_visit):
                        put_k(g, it)
                    it0 += n_visit

        @P.block.tensor
        def _(t):
            P.wait_phase(t)
            q_ctr.wait_abs(t, q_mark)

            def emit_pv(i):
                h, s, it, par, qc, pi, gi = ginfo[i]
                glist = phases[pi][5]
                ex.wait(t, i + 1)
                if mk_before[i]:
                    mkv.wait(t, mk_before[i])
                if pi >= 1 and i == phases[pi][3]:
                    if layer == 0:
                        fz.wait(t, FZB + pi * FPP)
                    else:
                        fz.wait(t, (pi - 1) * FPP + 1)
                if layer == 0:
                    if pi == 0:
                        vready.wait(t, 4 * (s + 1))
                    elif i == phases[pi][3]:
                        vready.wait(t, 16 if qc == 0 else 48)
                if layer == 1:
                    for spi in range(4):
                        if i == sp_first[spi]:
                            vgs.wait_abs(t, vg_base + 16 * sp_cum[spi])
                last = (i == phases[pi][4] - 1)
                lastslot = (i >= phases[pi][4] - len(glist))
                grp = glist[gi]
                ins = None
                for j, (sb, g) in enumerate(grp):
                    lastmm = last and (j == len(grp) - 1)
                    rhs = pst[:, (i % 4) * 1024 + j * 512:(i % 4) * 1024 + (j + 1) * 512]
                    if layer == 0:
                        st_f = (pi, g % 2) not in started
                        started.add((pi, g % 2))
                        stop_g = lastslot and sb == 3
                        vt = s * 4 + sb
                        ins = t.matmul(
                            accs[g % 2][:, :],
                            vhome(vt, h * 65, h * 65 + 65),
                            rhs, start=st_f, stop=stop_g)
                    else:
                        st_f = (pi, 0) not in started
                        started.add((pi, 0))
                        spi = pi // 8
                        bt = l1_bt(spi, s, sb)
                        t.matmul(
                            accs[pi % 2][:, :],
                            vhome(bt, (h % 8) * 128, (h % 8) * 128 + 128),
                            rhs, start=st_f, stop=lastmm)
                        st_f = (pi, 1) not in started
                        started.add((pi, 1))
                        ins = t.matmul(pl1[:, :], P.onesb[:, 0:1], rhs,
                                       start=st_f, stop=lastmm)
                pv.inc(ins)
                if last:
                    ng = 2 if layer == 0 else 1
                    for g in range(ng):
                        fz.wait(t, FZB + pi * FPP + g + 1)
                        if layer == 0:
                            ins2 = t.matmul(pbis[g][:, :], P.onesf[64:65, 0:64],
                                            rcs[64:65, g * 512:(g + 1) * 512],
                                            start=True, stop=True)
                        else:
                            ins2 = t.matmul(pbi[:, :], P.onesf[0:1, 0:128],
                                            rcs[0:1, 0:512],
                                            start=True, stop=True)
                        pb2.inc(ins2)

            started = set()
            for i in range(NGTOT):
                h, s, it, par, qc, pi, gi = ginfo[i]
                if gi == 0:
                    kvp.wait_item(t, it)
                if i >= 2:
                    ex.wait(t, i - 1)
                grp = phases[pi][5][gi]
                ins = None
                for j, (sb, g) in enumerate(grp):
                    cq = (h * 2 + g // 2) if layer == 0 else h
                    pbase = (g % 2) * 64 if layer == 0 else 0
                    ins = t.matmul(
                        ps2[i % 2][:, j * 512:(j + 1) * 512],
                        kst[pbase:pbase + kst_p,
                            par * 512 + sb * 128:par * 512 + (sb + 1) * 128],
                        P.q[pbase:pbase + kst_p,
                            cq * TOK + qc * 512:cq * TOK + (qc + 1) * 512],
                        start=True, stop=True)
                pe.inc(ins)
                if i >= 1:
                    emit_pv(i - 1)
            emit_pv(NGTOT - 1)

        @P.block.scalar
        def _(s_eng):
            P.wait_phase(s_eng)
            for i in range(NGTOT):
                h, s, it, par, qc, pi, gi = ginfo[i]
                pe.wait(s_eng, i + 1)
                if i >= 4:
                    pv.wait(s_eng, i - 3)
                ex.inc(s_eng.activation(
                    out=pst[:, (i % 4) * 1024:(i % 4 + 1) * 1024],
                    in_=ps2[i % 2][:, :], func=AF.Exp, scale=scale,
                    bias=P.biast[:, s:s + 1]))

        @P.block.vector
        def _(v):
            P.wait_phase(v)
            if at_guard is not None:
                at_guard[0].wait_abs(v, at_guard[1])
            if layer == 0:
                v.memset(P.xn[:, 0:NXT0 * VS0], 1.0)
                fz.inc(v.memset(vx0[:], 1.0))

                def copy_v(vt):
                    vrs.wait_item(v, vt)
                    fz.wait(v, FZB)
                    vready.inc(v.tensor_copy(
                        out=vhome(vt, 0, VS0)
                            .rearrange("p (h c) -> p h c", c=65)[:, :, 0:64],
                        in_=vraw[:, (vt % 4) * 512:(vt % 4 + 1) * 512]
                            .rearrange("p (h c) -> p h c", c=64)))
                for vt in range(16):
                    copy_v(vt)
            for i in range(NGTOT):
                h, s, it, par, qc, pi, gi = ginfo[i]
                if i in masked:
                    ex.wait(v, i + 1)
                    grp = phases[pi][5][gi]
                    ins = None
                    for j, (sb, g) in enumerate(grp):
                        sl = pst[:, (i % 4) * 1024 + j * 512:
                                 (i % 4) * 1024 + (j + 1) * 512]
                        ins = v.tensor_mul(sl, sl,
                                           P.masks[:, sb * 512:(sb + 1) * 512])
                    mkv.inc(ins)
                if layer == 0 and 4 <= i < 20:
                    copy_v(16 + 2 * (i - 4))
                    copy_v(17 + 2 * (i - 4))
                if i == phases[pi][4] - 1:
                    pv.wait(v, phases[pi][4])
                    if layer == 0:
                        gh = phases[pi][6]
                        fz.wait(v)
                        fz.inc(v.reciprocal(rcs[64:65, 0:512],
                                            accs[0][64:65, :]))
                        fz.inc(v.reciprocal(rcs[64:65, 512:1024],
                                            accs[1][64:65, :]))
                        for g in range(2):
                            pb2.wait(v, pi * 2 + g + 1)
                            fz.inc(v.tensor_copy(
                                out=bisb[0:64, g * 512:(g + 1) * 512],
                                in_=pbis[g][:, :]))
                            vd0 = (h * 4 + gh * 2 + g) * HD
                            ck, po = vd0 // 128, vd0 % 128
                            fz.wait(v)
                            fz.inc(v.tensor_mul(
                                P.at[po:po + HD,
                                     ck * TOK + qc * 512:ck * TOK + (qc + 1) * 512],
                                accs[g][0:HD, :],
                                bisb[0:64, g * 512:(g + 1) * 512]))
                    else:
                        fz.wait(v)
                        fz.inc(v.reciprocal(rcs[0:1, 0:512], pl1[0:1, :]))
                        pb2.wait(v, pi + 1)
                        fz.inc(v.tensor_copy(out=bisb[:, 0:512],
                                             in_=pbi[:, :]))
                        fz.wait(v)
                        fz.inc(v.tensor_mul(
                            P.at[:, h * TOK + qc * 512:h * TOK + (qc + 1) * 512],
                            accs[pi % 2][:, :], bisb[:, 0:512]))

        P.marks[f"at{layer}"] = (fz, fz.n)
        P.marks[f"qreads{layer}"] = (pe, pe.n)
        pdone = [(fz, fz.n), (pv, pv.n), (idxl, idxl.n)] + kvp.done()
        if layer == 0:
            pdone += vrs.done() + [(vready, vready.n)]
        else:
            pdone += [(vgs, vgs.n)]
        P.set_phase_done(pdone)
    P.tap(f"at{layer}", P.at[:], (128, HC * TOK))



def _emit_oproj(P, layer):
    nc = P.nc
    io = P.io
    at_ctr, at_mark = P.marks[f"at{layer}"]
    g_ctr, g_mark = P.marks[f"g{layer}"]
    w_dram = io["WO"] if layer == 0 else io["WOT"]
    wp = DStream(P, q="sw")
    mmq = P.ctr()
    pvv = P.ctr()
    with (nc.sbuf_tensor(f"ow{layer}", [128, 2 * HC * 128], BF16) as ow,
          nc.psum_tensor(f"op{layer}a", [128, 1024], F32) as poa,
          nc.psum_tensor(f"op{layer}b", [128, 1024], F32) as pob):
        po = [poa, pob]

        @P.block.gpsimd
        def _(g):
            P.wait_phase(g)
            for m in range(HC):
                if m >= 2:
                    mmq.wait(g, m - 1)
                wp.put(g.dma_start(
                    out=ow[:, (m % 2) * 2048:(m % 2 + 1) * 2048]
                        .rearrange("p (kc m) -> p kc m", m=128),
                    in_=w_dram[:, m * 128:(m + 1) * 128]
                        .rearrange("(kc p) m -> p kc m", p=128)), m)

        @P.block.tensor
        def _(t):
            P.wait_phase(t)
            at_ctr.wait_abs(t, at_mark)
            for m in range(HC):
                pm = m % 2
                wp.wait_item(t, m)
                if m >= 2:
                    pvv.wait(t, m - 1)
                ins = None
                for h2 in range(2):
                    for kc in range(HC):
                        ins = t.matmul(
                            po[pm][:, h2 * 512:(h2 + 1) * 512],
                            ow[:, pm * 2048 + kc * 128:pm * 2048 + (kc + 1) * 128],
                            P.at[:, kc * TOK + h2 * 512:kc * TOK + (h2 + 1) * 512],
                            start=(kc == 0), stop=(kc == HC - 1))
                mmq.inc(ins)

        @P.block.vector
        def _(v):
            P.wait_phase(v)
            g_ctr.wait_abs(v, g_mark)
            for m in range(HC):
                pm = m % 2
                mmq.wait(v, m + 1)
                bias = P.bo[:, m:m + 1] if layer == 0 else 0.0
                pvv.inc(v.scalar_tensor_tensor(
                    out=P.g[:, m * TOK:(m + 1) * TOK],
                    in0=po[pm][:, :], scalar=bias,
                    in1=P.g[:, m * TOK:(m + 1) * TOK],
                    op0=ALU.add, op1=ALU.mult))

        P.marks[f"p{layer}"] = (pvv, pvv.n)
        P.marks[f"atread{layer}"] = (mmq, mmq.n)
        P.set_phase_done([(pvv, pvv.n), (mmq, mmq.n)] + wp.done())
    P.tap(f"p{layer}", P.g[:], (128, HC * TOK))



def _emit_combine(P, layer):
    nc = P.nc
    io = P.io
    p_ctr, p_mark = P.marks[f"p{layer}"]
    x_src = io["XOWN"] if layer == 0 else io["XRES"]
    x_dst = io["XRES"] if layer == 0 else io["OUT"]
    tt = P.ctr()
    lx = DStream(P, q="hw")
    av = P.ctr()
    sw = DStream(P, q="hw")
    with (nc.sbuf_tensor(f"cb{layer}x", [128, 2 * D], F32) as xc,
          nc.sbuf_tensor(f"cb{layer}p", [128, 2 * D], BF16) as ptm,
          nc.sbuf_tensor(f"cb{layer}o", [128, 2 * D], F32) as xo,
          nc.psum_tensor(f"cb{layer}t", [128, 2048], BF16) as tp):

        @P.block.tensor
        def _(t):
            P.wait_phase(t)
            p_ctr.wait_abs(t, p_mark)
            for H in range(2 * NT):
                i, hf = H // 2, H % 2
                if H >= 2:
                    done_tiles, done_halves = (H - 2) // 2, (H - 2) % 2 + 1
                    av.wait(t, 3 * done_tiles + done_halves)
                ins = None
                for k in range(8):
                    kc = hf * 8 + k
                    ins = t.transpose(
                        tp[:, (H % 2) * 1024 + k * 128:(H % 2) * 1024 + (k + 1) * 128],
                        P.g[:, kc * TOK + i * 128:kc * TOK + (i + 1) * 128],
                        P.identb[:, :])
                tt.inc(ins)

        @P.block.sync
        def _(sy):
            P.wait_phase(sy)
            for i in range(NT):
                if i >= 2:
                    av.wait(sy, 3 * (i - 1))
                    sw.put(sy.dma_start(
                        out=x_dst[(i - 2) * 128:(i - 1) * 128, :],
                        in_=xo[:, (i % 2) * D:(i % 2 + 1) * D]), i - 2)
                lx.put(sy.dma_start(
                    out=xc[:, (i % 2) * D:(i % 2 + 1) * D],
                    in_=x_src[i * 128:(i + 1) * 128, :]), i)
            for i in range(NT - 2, NT):
                av.wait(sy, 3 * (i + 1))
                sw.put(sy.dma_start(
                    out=x_dst[i * 128:(i + 1) * 128, :],
                    in_=xo[:, (i % 2) * D:(i % 2 + 1) * D]), i)

        @P.block.vector
        def _(v):
            P.wait_phase(v)
            for i in range(NT):
                p = i % 2
                for hf in range(2):
                    H = 2 * i + hf
                    tt.wait(v, H + 1)
                    if i >= 2 and hf == 0:
                        av.wait(v, 3 * (i - 2) + 3)
                    ins = None
                    for k in range(8):
                        kc = hf * 8 + k
                        ins = v.tensor_copy(
                            out=ptm[:, p * D + kc * 128:p * D + (kc + 1) * 128],
                            in_=tp[:, (H % 2) * 1024 + k * 128:
                                   (H % 2) * 1024 + (k + 1) * 128])
                    av.inc(ins)
                lx.wait_item(v, i)
                av.wait(v)
                if i >= 2:
                    sw.wait_item(v, i - 2)
                av.inc(v.tensor_add(xo[:, p * D:(p + 1) * D],
                                    xc[:, p * D:(p + 1) * D],
                                    ptm[:, p * D:(p + 1) * D]))

        if layer == 0:
            P.marks["xresw"] = sw
        else:
            P.marks.setdefault("final_waits", []).extend(sw.done())
        P.marks[f"gread{layer}"] = (tt, tt.n)
        P.protect(tt, *sw.ctrs)
        P.set_phase_done([(av, av.n), (tt, tt.n)] + sw.done() + lx.done())



def _emit_qkv1(P):
    nc = P.nc
    io = P.io
    xn_ctr, xn_mark = P.marks["xn1"]
    qfree = P.marks["qreads0"]
    wp = DStream(P, q="sw")
    ev = P.ctr()
    mmq = P.ctr()
    tvt = P.ctr()
    vte = P.ctr()
    kscK = DStream(P, q="hw")
    kscV = DStream(P, per=128, q="hw")
    with ExitStack() as _es:
        qw = _es.enter_context(nc.sbuf_tensor("q1w", [128, 2 * HC * 128], BF16))
        kstage = _es.enter_context(nc.sbuf_tensor("k1st", [128, 2 * 1024], BF16))
        vfm = _es.enter_context(nc.sbuf_tensor("v1fm", [128, 2 * 1024], BF16))
        vtm = _es.enter_context(nc.sbuf_tensor("v1tm", [128, 2 * 1024], BF16))
        pqa = _es.enter_context(nc.psum_tensor("q1pa", [128, 1024], F32))
        pqb = _es.enter_context(nc.psum_tensor("q1pb", [128, 1024], F32))
        tpv = _es.enter_context(nc.psum_tensor("v1tp", [128, 1024], BF16))
        pq = [pqa, pqb]

        @P.block.gpsimd
        def _(g):
            P.wait_phase(g)
            for m in range(48):
                if m >= 2:
                    mmq.wait(g, m - 1)
                wp.put(g.dma_start(
                    out=qw[:, (m % 2) * 2048:(m % 2 + 1) * 2048]
                        .rearrange("p (kc m) -> p kc m", m=128),
                    in_=io["WQKVT"][:, m * 128:(m + 1) * 128]
                        .rearrange("(kc p) m -> p kc m", p=128)), m)

        @P.block.tensor
        def _(t):
            P.wait_phase(t)
            xn_ctr.wait_abs(t, xn_mark)
            for m in range(48):
                pm = m % 2
                wp.wait_item(t, m)
                if m >= 2:
                    ev.wait(t, m - 1)
                ins = None
                for h2 in range(2):
                    for kc in range(HC):
                        ins = t.matmul(
                            pq[pm][:, h2 * 512:(h2 + 1) * 512],
                            qw[:, pm * 2048 + kc * 128:pm * 2048 + (kc + 1) * 128],
                            P.xn[:, kc * TOK + h2 * 512:kc * TOK + (h2 + 1) * 512],
                            start=(kc == 0), stop=(kc == HC - 1))
                mmq.inc(ins)
                if 32 < m <= 47:
                    _qkv1_vtransp(P, t, m - 1 - 32, vfm, tpv, ev, tvt, vte)
            _qkv1_vtransp(P, t, 15, vfm, tpv, ev, tvt, vte)

        @P.block.scalar
        def _(s):
            P.wait_phase(s)
            qfree[0].wait_abs(s, qfree[1])
            for m in range(48):
                pm = m % 2
                mmq.wait(s, m + 1)
                if m < 16:
                    ev.inc(s.activation(out=P.q[:, m * TOK:(m + 1) * TOK],
                                        in_=pq[pm][:, :], func=AF.Copy))
                elif m < 32:
                    mk = m - 16
                    if mk >= 2:
                        kscK.wait_item(s, mk - 2)
                    ev.inc(s.activation(
                        out=kstage[:, (mk % 2) * 1024:(mk % 2 + 1) * 1024],
                        in_=pq[pm][:, :], func=AF.Copy))
                else:
                    mv = m - 32
                    if mv >= 2:
                        tvt.wait(s, mv - 1)
                    ev.inc(s.activation(
                        out=vfm[:, (mv % 2) * 1024:(mv % 2 + 1) * 1024],
                        in_=pq[pm][:, :], func=AF.Copy))

        @P.block.vector
        def _(v):
            P.wait_phase(v)
            for mv in range(16):
                tvt.wait(v, mv + 1)
                if mv >= 2:
                    kscV.wait_item(v, mv - 2)
                ins = None
                for k in range(8):
                    ins = v.tensor_copy(
                        out=vtm[:, (mv % 2) * 1024 + k * 128:
                                (mv % 2) * 1024 + (k + 1) * 128],
                        in_=tpv[:, k * 128:(k + 1) * 128])
                vte.inc(ins)

        @P.block.sync
        def _(sy):
            P.wait_phase(sy)
            for mk in range(16):
                ev.wait(sy, 16 + mk + 1)
                kscK.put(sy.dma_start(
                    out=io["CKI1"][mk * 128:(mk + 1) * 128, :],
                    in_=kstage[:, (mk % 2) * 1024:(mk % 2 + 1) * 1024]), mk)
            for mv in range(16):
                vte.wait(sy, mv + 1)
                for ts_i in range(8):
                    kscV.put(sy.dma_start(
                        out=io["CVI1"][ts_i * 128:(ts_i + 1) * 128,
                                       mv * 128:(mv + 1) * 128],
                        in_=vtm[:, (mv % 2) * 1024 + ts_i * 128:
                                (mv % 2) * 1024 + (ts_i + 1) * 128]), mv)

        @P.block.gpsimd
        def _(g):
            P.wait_phase(g)
            kscK.wait_all(g)
            kscV.wait_all(g)
            P.cc.inc(g.collective_compute(
                "AllGather", ALU.bypass, replica_groups=[list(range(NCORE))],
                ins=[io["CKI1"].opt()], outs=[io["CKO1"].opt()]), 1)
            P.cc.inc(g.collective_compute(
                "AllGather", ALU.bypass, replica_groups=[list(range(NCORE))],
                ins=[io["CVI1"].opt()], outs=[io["CVO1"].opt()]), 1)

        P.marks["q1"] = (ev, ev.base + 16)
        P.marks["cc1"] = (P.cc, P.cc.n)
        P.protect(ev)
        P.set_phase_done([(mmq, mmq.n), (ev, ev.n), (tvt, tvt.n),
                          (vte, vte.n)] + kscK.done() + kscV.done() + wp.done())
    P.tap("q1", P.q[:], (128, HC * TOK))


def _qkv1_vtransp(P, t, mv, vfm, tpv, ev, tvt, vte):
    ev.wait(t, 32 + mv + 1)
    if mv >= 1:
        vte.wait(t, mv)
    ins = None
    for ts_i in range(8):
        ins = t.transpose(
            tpv[:, ts_i * 128:(ts_i + 1) * 128],
            vfm[:, (mv % 2) * 1024 + ts_i * 128:(mv % 2) * 1024 + (ts_i + 1) * 128],
            P.identb[:, :])
    tvt.inc(ins)



def _emit_ffn1(P):
    nc = P.nc
    io = P.io
    xn_ctr, xn_mark = P.marks["xn1"]
    gfree = P.marks["gread0"]
    csx = P.ctr()
    with (nc.psum_tensor("f1cs0", [1, 512], F32) as pcs0,
          nc.psum_tensor("f1cs1", [1, 512], F32) as pcs1):
        pcs = [pcs0, pcs1]

        @P.block.tensor
        def _(t):
            P.wait_phase(t)
            xn_ctr.wait_abs(t, xn_mark)
            P.pre_v.wait(t)
            ins = None
            for h2 in range(2):
                for kc in range(HC):
                    ins = t.matmul(pcs[h2][0:1, :], P.onesb[:, 0:1],
                                   P.xn[:, kc * TOK + h2 * 512:kc * TOK + (h2 + 1) * 512],
                                   start=(kc == 0), stop=(kc == HC - 1))
            csx.inc(ins)

        @P.block.vector
        def _(v):
            P.wait_phase(v)
            csx.wait(v, 1)
            v.tensor_copy(out=P.csum[0:1, 0:512], in_=pcs0[0:1, :])
            csx.inc(v.tensor_copy(out=P.csum[0:1, 512:1024], in_=pcs1[0:1, :]))

    wg1 = DStream(P, nbuf=3, per=32, q="sw")
    icm = P.ctr()
    sev = P.ctr()
    uev = P.ctr()
    actw = DStream(P, nbuf=4, q="hw")
    csa = P.ctr()
    with ExitStack() as _es:
        gtb = _es.enter_context(nc.sbuf_tensor("g1w", [128, 3 * HC * 128], BF16))
        utb = _es.enter_context(nc.sbuf_tensor("u1w", [128, 3 * HC * 128], BF16))
        sgb = _es.enter_context(nc.sbuf_tensor("sg1", [128, 4 * 512], F32))
        usb = _es.enter_context(nc.sbuf_tensor("us1", [128, 4 * 512], F32))
        acb = _es.enter_context(nc.sbuf_tensor("ac1", [128, 4 * 512], BF16))
        pgh = [_es.enter_context(nc.psum_tensor(f"f1g{m}", [128, 512], F32))
               for m in range(3)]
        puh = [_es.enter_context(nc.psum_tensor(f"f1u{m}", [128, 512], F32))
               for m in range(3)]
        pa0 = _es.enter_context(nc.psum_tensor("f1a0", [1, 512], F32))
        pa1 = _es.enter_context(nc.psum_tensor("f1a1", [1, 512], F32))
        pah = [pa0, pa1]

        @P.block.gpsimd
        def _(g):
            P.wait_phase(g)
            for ic in range(L1_IC):
                if ic >= 3:
                    icm.wait(g, 2 * (ic - 2))
                wg1.put(g.dma_start(
                    out=gtb[:, (ic % 3) * 2048:(ic % 3 + 1) * 2048]
                        .rearrange("p (kc m) -> p kc m", m=128),
                    in_=io["GT"][:, ic * 128:(ic + 1) * 128]
                        .rearrange("(kc p) m -> p kc m", p=128)), ic)
                wg1.put(g.dma_start(
                    out=utb[:, (ic % 3) * 2048:(ic % 3 + 1) * 2048]
                        .rearrange("p (kc m) -> p kc m", m=128),
                    in_=io["UT"][:, ic * 128:(ic + 1) * 128]
                        .rearrange("(kc p) m -> p kc m", p=128)), ic)

        @P.block.tensor
        def _(t):
            P.wait_phase(t)
            csx.wait(t)
            P.pre_v.wait(t)
            for ic in range(L1_IC):
                pm = ic % 3
                wg1.wait_item(t, ic)
                for h2 in range(2):
                    n = 2 * ic + h2
                    if n >= 3:
                        sev.wait(t, n - 2)
                    for kc in range(HC):
                        t.matmul(pgh[n % 3][:, :],
                                 gtb[:, pm * 2048 + kc * 128:pm * 2048 + (kc + 1) * 128],
                                 P.xn[:, kc * TOK + h2 * 512:kc * TOK + (h2 + 1) * 512],
                                 start=(kc == 0), stop=False)
                    t.matmul(pgh[n % 3][:, :], P.negf[0:1, :],
                             P.csum[0:1, h2 * 512:(h2 + 1) * 512],
                             start=False, stop=True)
                    if ic >= 1:
                        uev.wait(t, 2 * (n - 1))
                        t.matmul(pah[h2][0:1, :], P.onesb[:, 0:1],
                                 acb[:, ((n - 2) % 4) * 512:
                                     ((n - 2) % 4 + 1) * 512],
                                 start=(ic == 1), stop=False)
                    for kc in range(HC):
                        t.matmul(puh[n % 3][:, :],
                                 utb[:, pm * 2048 + kc * 128:pm * 2048 + (kc + 1) * 128],
                                 P.xn[:, kc * TOK + h2 * 512:kc * TOK + (h2 + 1) * 512],
                                 start=(kc == 0), stop=False)
                    ins = t.matmul(puh[n % 3][:, :], P.negf[0:1, :],
                                   P.csum[0:1, h2 * 512:(h2 + 1) * 512],
                                   start=False, stop=True)
                    icm.inc(ins)
            for h2 in range(2):
                uev.wait(t, 2 * (2 * (L1_IC - 1) + h2 + 1))
                ins = t.matmul(pah[h2][0:1, :], P.onesb[:, 0:1],
                               acb[:, ((2 * (L1_IC - 1) + h2) % 4) * 512:
                                   ((2 * (L1_IC - 1) + h2) % 4 + 1) * 512],
                               start=False, stop=True)
            csa.inc(ins)

        @P.block.scalar
        def _(s):
            P.wait_phase(s)
            for ic in range(L1_IC):
                for h2 in range(2):
                    n = 2 * ic + h2
                    icm.wait(s, n + 1)
                    if n >= 4:
                        sev.wait(s, n - 3)
                    sev.inc(s.activation(
                        out=sgb[:, (n % 4) * 512:(n % 4 + 1) * 512],
                        in_=pgh[n % 3][:, :], func=AF.Silu,
                        scale=P.gsc[:, ic:ic + 1]))

        @P.block.vector
        def _(v):
            P.wait_phase(v)
            for ic in range(L1_IC):
                for h2 in range(2):
                    n = 2 * ic + h2
                    icm.wait(v, n + 1)
                    sev.wait(v, n + 1)
                    if n >= 4:
                        actw.wait_item(v, n - 4)
                        uev.wait(v, 2 * (n - 3))
                    uev.inc(v.tensor_scalar(
                        out=usb[:, (n % 4) * 512:(n % 4 + 1) * 512],
                        in0=puh[n % 3][:, :], scalar1=P.usc[:, ic:ic + 1],
                        scalar2=None, op0=ALU.mult))
                    uev.wait(v)
                    uev.inc(v.tensor_mul(
                        acb[:, (n % 4) * 512:(n % 4 + 1) * 512],
                        sgb[:, (n % 4) * 512:(n % 4 + 1) * 512],
                        usb[:, (n % 4) * 512:(n % 4 + 1) * 512]))

        @P.block.sync
        def _(sy):
            P.wait_phase(sy)
            for ic in range(L1_IC):
                for h2 in range(2):
                    n = 2 * ic + h2
                    uev.wait(sy, 2 * (n + 1))
                    actw.put(sy.dma_start(
                        out=io["ACT"][ic * 128:(ic + 1) * 128,
                                      h2 * 512:(h2 + 1) * 512],
                        in_=acb[:, (n % 4) * 512:(n % 4 + 1) * 512]), n)

        @P.block.vector
        def _(v):
            P.wait_phase(v)
            csa.wait(v, 1)
            v.tensor_copy(out=P.csum[0:1, 1024:1536], in_=pa0[0:1, :])
            csa.inc(v.tensor_copy(out=P.csum[0:1, 1536:2048], in_=pa1[0:1, :]))

    dwp = DStream(P, nbuf=4, q="sw")
    arp = DStream(P, nbuf=4, q="hw")
    dmt = P.ctr()
    dev = P.ctr()
    with ExitStack() as _es:
        dtb = _es.enter_context(nc.sbuf_tensor("d1w", [128, 4 * 512], BF16))
        arb = _es.enter_context(nc.sbuf_tensor("d1a", [128, 4 * 1024], BF16))
        pd0 = _es.enter_context(nc.psum_tensor("f1d0", [128, 1024], F32))
        pd1 = _es.enter_context(nc.psum_tensor("f1d1", [128, 1024], F32))
        pd2 = _es.enter_context(nc.psum_tensor("f1d2", [128, 1024], F32))
        pd3 = _es.enter_context(nc.psum_tensor("f1d3", [128, 1024], F32))
        pd = [pd0, pd1, pd2, pd3]

        @P.block.gpsimd
        def _(g):
            P.wait_phase(g)
            icm.wait(g); sev.wait(g); uev.wait(g); csa.wait(g)
            actw.wait_all(g)
            for qd in range(4):
                for ic in range(L1_IC):
                    n = qd * L1_IC + ic
                    if n >= 4:
                        pq, pi = divmod(n - 4, L1_IC)
                        dmt.wait(g, pq * (L1_IC + 1) + pi + 1)
                    dwp.put(g.dma_start(
                        out=dtb[:, (n % 4) * 512:(n % 4 + 1) * 512],
                        in_=io["DT"][ic * 128:(ic + 1) * 128,
                                     qd * 512:(qd + 1) * 512]), n)

        @P.block.sync
        def _(sy):
            P.wait_phase(sy)
            icm.wait(sy); sev.wait(sy); uev.wait(sy); csa.wait(sy)
            for qd in range(4):
                for ic in range(L1_IC):
                    n = qd * L1_IC + ic
                    if n >= 4:
                        pq, pi = divmod(n - 4, L1_IC)
                        dmt.wait(sy, pq * (L1_IC + 1) + pi + 1)
                    if qd == 0:
                        actw.wait_item(sy, 2 * ic)
                        actw.wait_item(sy, 2 * ic + 1)
                    arp.put(sy.dma_start(
                        out=arb[:, (n % 4) * 1024:(n % 4 + 1) * 1024],
                        in_=io["ACT"][ic * 128:(ic + 1) * 128, :]), n)

        @P.block.tensor
        def _(t):
            P.wait_phase(t)
            csa.wait(t)
            for qd in range(4):
                if qd >= 1:
                    dev.wait(t, 4 * qd)
                for ic in range(L1_IC):
                    n = qd * L1_IC + ic
                    dwp.wait_item(t, n)
                    arp.wait_item(t, n)
                    ins = None
                    for m in range(4):
                        for h2 in range(2):
                            ins = t.matmul(
                                pd[m][:, h2 * 512:(h2 + 1) * 512],
                                dtb[:, (n % 4) * 512 + m * 128:(n % 4) * 512 + (m + 1) * 128],
                                arb[:, (n % 4) * 1024 + h2 * 512:(n % 4) * 1024 + (h2 + 1) * 512],
                                start=(ic == 0), stop=False)
                    dmt.inc(ins)
                ins = None
                for m in range(4):
                    for h2 in range(2):
                        ins = t.matmul(pd[m][:, h2 * 512:(h2 + 1) * 512],
                                       P.negf[0:1, :],
                                       P.csum[0:1, 1024 + h2 * 512:1024 + (h2 + 1) * 512],
                                       start=False, stop=True)
                dmt.inc(ins)

        @P.block.scalar
        def _(s):
            P.wait_phase(s)
            gfree[0].wait_abs(s, gfree[1])
            for qd in range(4):
                dmt.wait(s, (qd + 1) * (L1_IC + 1))
                for m in range(4):
                    hc = qd * 4 + m
                    dev.inc(s.activation(
                        out=P.g[:, hc * TOK:(hc + 1) * TOK],
                        in_=pd[m][:, :], func=AF.Silu,
                        scale=P.dsc[:, hc:hc + 1]))

        P.marks["g1"] = (dev, dev.n)
        P.marks["xnreads1"] = (dmt, dmt.n)
        P.protect(dev, dmt)
        P.set_phase_done([(dev, dev.n), (dmt, dmt.n)] + dwp.done() + arp.done())
    P.tap("g1", P.g[:], (128, HC * TOK))



_PROG_CACHE = {}


def kernel(**inputs):
    per_core, com = _host_prepare(inputs)
    if "prog" not in _PROG_CACHE:
        _PROG_CACHE["prog"] = build_program()
    nc = _PROG_CACHE["prog"]
    in_maps = []
    for c in range(NCORE):
        m = dict(com)
        m.update(per_core[c])
        in_maps.append(m)
    res = run_bass_kernel_spmd(nc, in_maps, core_ids=list(range(NCORE)))
    out = np.empty((2, NCH * CH, D), dtype=np.float32)
    for c in range(NCORE):
        b, jq = c // 4, c % 4
        j2 = NCH - 1 - jq
        o = res.results[c]["OUT"]
        out[b, jq * CH:(jq + 1) * CH, :] = o[0:CH, :]
        out[b, j2 * CH:(j2 + 1) * CH, :] = o[CH:2 * CH, :]
    return out
